# version 03
# baseline (speedup 1.0000x reference)
import numpy as np
import ml_dtypes
import concourse.bass as bass
import concourse.mybir as mybir
from concourse.bass_utils import run_bass_kernel_spmd

F32 = mybir.dt.float32
BF16 = mybir.dt.bfloat16
AF = mybir.ActivationFunctionType
ALU = mybir.AluOpType
AX = mybir.AxisListType

NC = 8
TOK = 2048
NT = 16
H = 1024
NK = 128
NB = 4
I_REAL = 2816
NI = 22
BLK_I = [384, 384, 384, 384, 384, 384, 256, 256]
BLK_OFF = [0, 384, 768, 1152, 1536, 1920, 2304, 2560]
WSH = 384
NST = 4
MAGIC = 12582912.0
TWO_PI = 6.283185307179586
INV_2PI = float(np.float32(1.0 / TWO_PI))


def _split_2pi():
    c1 = float(np.float32(np.ldexp(np.floor(np.ldexp(TWO_PI, 10)), -10)))
    r1 = TWO_PI - c1
    c2 = float(np.float32(np.ldexp(np.floor(np.ldexp(r1, 22)), -22)))
    c3 = float(np.float32(TWO_PI - c1 - c2))
    return c1, c2, c3


CW1, CW2, CW3 = _split_2pi()
HALF_PI = float(np.float32(np.pi / 2))
PI_F = float(np.float32(np.pi))

import concourse.dve_ops as dvo
from concourse.dve_op import DveOp
from concourse.dve_spec import Spec, Src0, Src1, C0, C1, C2, Zero, minn, maxx, lower


def _register(name, spec, reference, row, perf_en=None):
    for o in dvo.OPS:
        if o.name == name:
            return o
    pe = perf_en or {}
    compiled = lower(spec, ver="v3", perf_en=pe.get("v3", False), name=name)
    op = DveOp(name, spec, subdim=False, uops_sha={"v3": compiled.sha("v3")},
               perf_en=pe, reference=reference)
    dvo.OPS.append(op)
    dvo._SUB_OPCODE_FOR_NAME[name] = row
    dvo.CUSTOM_DVE_SPECS[name] = spec
    dvo.CUSTOM_DVE_REFS[name] = reference
    return op


def _rni(x, M):
    return (x + M) - M


OPA = _register(
    "FP4A", Spec(body=maxx(minn(_rni(Src0 + Src0, C0), C1), Zero - C1) * C2),
    lambda in0, in1, s0, s1, imm2: np.clip(np.round(2.0 * in0), -s1, s1) * imm2,
    row=30)
_Rn = _rni(Src0, C0)
_cA = maxx(minn(_Rn, C1), Zero - C1)
_cB = maxx(minn(_cA, C2), Zero - C2)
OPB = _register(
    "FP4B", Spec(body=(_cA - _cB) + Src1),
    lambda in0, in1, s0, s1, imm2: np.clip(np.round(in0), -s1, s1)
    - np.clip(np.round(in0), -imm2, imm2) + in1,
    row=31)
_Wn = _rni(Src0 * C1, C0)
OPC = _register(
    "FP4C", Spec(body=(_Wn + _Wn) + Src1),
    lambda in0, in1, s0, s1, imm2: 2.0 * np.round(s1 * in0) + in1,
    row=29)
OPR = _register(
    "RNIMUL", Spec(body=_rni(Src0 * C1, C0)),
    lambda in0, in1, s0, s1, imm2: np.round(in0 * s1),
    row=28)


class Sem:
    def __init__(self, h):
        self.h = h
        self.n = 0


class _InstProxy:

    def __init__(self, inst, eng):
        self._i = inst
        self._e = eng

    def then_inc(self, sem, k=1):
        e = self._e
        if any(sem is g for g in e._guard):
            tgt = self._i if self._i is e._last_real or e._last_real is None \
                else e._last_real
            tgt.then_inc(e._fl, 1)
            e._fln[0] += 1
            e._raw.wait_ge(e._fl, e._fln[0])
            return e._raw.nop().then_inc(sem, k)
        return self._i.then_inc(sem, k)

    def __getattr__(self, name):
        return getattr(self._i, name)


class EngProxy:

    _NONREAL = {"wait_ge", "nop", "engine_nop"}

    def __init__(self, eng, fl, fln, guard):
        self._raw = eng
        self._fl = fl
        self._fln = fln
        self._guard = guard
        self._last_real = None

    def __getattr__(self, name):
        attr = getattr(self._raw, name)
        if not callable(attr):
            return attr

        def call(*a, **kw):
            r = attr(*a, **kw)
            if hasattr(r, "then_inc"):
                if name not in EngProxy._NONREAL:
                    self._last_real = r
                return _InstProxy(r, self)
            return r
        return call


def build_core(dbg=()):
    nc = bass.Bass("TRN2", target_bir_lowering=False,
                   detect_race_conditions=False)

    def din(name, shape, dt):
        return nc.dram_tensor(name, shape, dt, kind="ExternalInput").ap()

    def dout(name, shape, dt):
        return nc.dram_tensor(name, shape, dt, kind="ExternalOutput").ap()

    def dint(name, shape, dt, space="Local"):
        return nc.dram_tensor(name, shape, dt, kind="Internal",
                              addr_space=space).ap()

    x_in = din("x_in", [TOK, H], F32)
    ident_in = din("ident_in", [128, 128], BF16)
    jrev_in = din("jrev_in", [128, 128], BF16)
    ones1_in = din("ones1_in", [1, 128], F32)
    dramp_in = din("dramp_in", [1, 129], F32)
    logdt_in = din("logdt_in", [128, 1], F32)
    dcol_in = din("dcol_in", [128, 1], F32)
    cre_in = din("cre_in", [128, 32], F32)
    cim_in = din("cim_in", [128, 32], F32)
    wlr_in = din("wlr_in", [1, 32], F32)
    wim_in = din("wim_in", [1, 32], F32)
    wout_in = din("wout_in", [1024, 1024], F32)
    bout_in = din("bout_in", [1, 1024], F32)
    wg_in = din("wg_in", [WSH, 1024], F32)
    wu_in = din("wu_in", [WSH, 1024], F32)
    wd_in = din("wd_in", [1024, WSH], F32)

    out_d = dout("out", [TOK, H], F32)
    dbg_d = {}
    for name, shape, dt in [
        ("xnbf", [128, NT * 1024], BF16),
        ("xq", [128, NT * 1024], BF16),
        ("urev", [128, NK * 128], BF16),
        ("s2b", [128, NB * 32 * 64], BF16),
        ("gy", [128, 128 * 128], BF16),
        ("kext", [128, 256], BF16),
        ("hst0", [128, 4 * I_REAL], BF16),
        ("g", [128, NT * 1024], BF16),
        ("ln0", [128, 2064], F32),
        ("wgu0", [128, 2 * 8 * WSH], BF16),
        ("xqs0", [128, 4 * 1024], BF16),
        ("hf0", [128, 4 * I_REAL], F32),
    ]:
        if name in dbg:
            dbg_d[name] = dout("dbg_" + name, shape, dt)
        elif name in ("wgu0", "xqs0", "hf0", "hst0", "g"):
            dbg_d[name] = dint("dbg_" + name, shape, dt)

    agx_src = dint("agx_src", [TOK, H], BF16)
    agx_dst = dint("agx_dst", [NC * TOK, H], BF16, "Shared")
    agy_src = dint("agy_src", [128, NC * TOK], BF16)
    agy_dst = dint("agy_dst", [NC * 128, NC * TOK], BF16, "Shared")
    agg_src = dint("agg_src", [1024, WSH], BF16)
    agg_dst = dint("agg_dst", [NC * 1024, WSH], BF16, "Shared")
    agu_src = dint("agu_src", [1024, WSH], BF16)
    agu_dst = dint("agu_dst", [NC * 1024, WSH], BF16, "Shared")
    agd_src = dint("agd_src", [WSH, 1024], BF16)
    agd_dst = dint("agd_dst", [NC * WSH, 1024], BF16, "Shared")
    lam2r_d = dint("lam2r_d", [128, 128 * 32], BF16)
    lam2i_d = dint("lam2i_d", [128, 128 * 32], BF16)
    e2r_d = dint("e2r_d", [128, 32 * 129], BF16)
    e2i_d = dint("e2i_d", [128, 32 * 129], BF16)
    kext_d = dint("kext_d", [128, 256], BF16)
    xq_d = dint("xq_d", [128, NT * 1024], BF16)

    from contextlib import ExitStack
    root = ExitStack()
    SCHED = []

    def on(engine):
        def deco(fn):
            SCHED.append((engine, fn))
            return None
        return deco

    def sbuf(st, name, shape, dt):
        return st.enter_context(nc.sbuf_tensor(name, shape, dt))

    with root:
        ident_sb = sbuf(root, "ident_sb", [128, 128], BF16)
        jrev_sb = sbuf(root, "jrev_sb", [128, 128], BF16)
        ones1_sb = sbuf(root, "ones1_sb", [1, 128], F32)
        bout_sb = sbuf(root, "bout_sb", [1, 1024], F32)
        sm = sbuf(root, "sm", [128, 16], F32)
        smq = sbuf(root, "smq", [128, 176], F32)
        smq2 = sbuf(root, "smq2", [128, 176], F32)
        smq3 = sbuf(root, "smq3", [128, 176], F32)
        ug_buf = sbuf(root, "ug_buf", [128, NK * 128], BF16)
        mtr_sb = sbuf(root, "mtr_sb", [128, 256], F32)
        mti_sb = sbuf(root, "mti_sb", [128, 256], F32)
        PS = [root.enter_context(nc.psum_tensor(f"ps{i}", [128, 512], F32))
              for i in range(8)]
        PSB = [p[:].bitcast(BF16) for p in PS]

        LD = Sem(root.enter_context(nc.semaphore("LD")))
        VV = Sem(root.enter_context(nc.semaphore("VV")))
        AA = Sem(root.enter_context(nc.semaphore("AA")))
        TT = Sem(root.enter_context(nc.semaphore("TT")))
        CX = Sem(root.enter_context(nc.semaphore("CX")))
        CG = Sem(root.enter_context(nc.semaphore("CG")))
        CU = Sem(root.enter_context(nc.semaphore("CU")))
        CD = Sem(root.enter_context(nc.semaphore("CD")))
        CY = Sem(root.enter_context(nc.semaphore("CY")))
        ST = Sem(root.enter_context(nc.semaphore("ST")))
        GY = Sem(root.enter_context(nc.semaphore("GY")))
        RD = Sem(root.enter_context(nc.semaphore("RD")))

        bank_guard = {i: None for i in range(8)}
        _pid_cache = {}

        def get_pid(e):
            if "v" not in _pid_cache:
                p = e.partition_id()
                _pid_cache["v"] = (p * 128, p * TOK)
            return _pid_cache["v"]

        def pe_wait_bank(e, bg):
            if bg is not None:
                e.wait_ge(bg[0].h, bg[1])

        def guard_bank(bank, sem):
            bank_guard[bank] = (sem, sem.n)

        @on("sync")
        def _(e):
            for dst, src in [(ident_sb[:], ident_in), (jrev_sb[:], jrev_in),
                             (ones1_sb[:], ones1_in), (bout_sb[:], bout_in)]:
                e.dma_start(out=dst, in_=src).then_inc(LD.h, 16)
        LD.n += 64
        ld_consts = LD.n

        @on("vector")
        def _(e):
            e.memset(sm[:, 10:11], 1e-5).then_inc(RD.h, 1)
        RD.n += 1
        EPS_RD = RD.n


        p1 = ExitStack()
        xt_sb = [sbuf(p1, f"xt{i}", [128, 1024], F32) for i in range(2)]
        xn_f32 = sbuf(p1, "xn_f32", [128, 1024], F32)
        sqs_sb = sbuf(p1, "sqs_sb", [128, 1024], F32)
        qA = sbuf(p1, "qA", [128, 1024], F32)
        qta = sbuf(p1, "qta", [128, 1024], BF16)
        qtb = sbuf(p1, "qtb", [128, 1024], BF16)
        xnbf = sbuf(p1, "xnbf", [128, NT * 1024], BF16)
        xq_fm = sbuf(p1, "xq_fm", [128, NT * 1024], BF16)
        xqt = sbuf(p1, "xqt", [128, 1024], BF16)
        wqt = sbuf(p1, "wqt", [128, 1024], BF16)
        gT_sb = sbuf(p1, "gT_sb", [128, 8 * WSH], BF16)
        dT_sb = sbuf(p1, "dT_sb", [128, 3 * 1024], BF16)

        def qdq_emit(e, src_ap, ncols, out_bf_ap, rd):
            nb = ncols // 16
            s3 = src_ap.rearrange("p (b s) -> p b s", s=16)
            e.tensor_reduce(smq[:, 0:nb].unsqueeze(2), s3, AX.X, ALU.max,
                            apply_absolute_value=True).then_inc(RD.h, 1)
            e.wait_ge(RD.h, rd[0])
            e.tensor_scalar(smq2[:, 0:nb], smq[:, 0:nb], 1.0 / 6.0, 1e-12,
                            ALU.mult, ALU.max).then_inc(RD.h, 1)
            e.wait_ge(RD.h, rd[1])
            e.reciprocal(smq3[:, 0:nb], smq2[:, 0:nb]).then_inc(RD.h, 1)
            e.wait_ge(RD.h, rd[2])
            rexp = smq3[:, 0:nb].unsqueeze(2).broadcast_to([128, nb, 16])
            e.tensor_tensor(qA[:, 0:ncols].rearrange("p (b s) -> p b s", s=16),
                            s3, rexp, ALU.mult)
            e._custom_dve(OPA, out=qta[:, 0:ncols], in0=qA[:, 0:ncols],
                          s0=MAGIC, s1=4.0, imm2=0.5)
            e._custom_dve(OPB, out=qtb[:, 0:ncols], in0=qA[:, 0:ncols],
                          in1=qta[:, 0:ncols], s0=MAGIC, s1=4.0, imm2=2.0)
            e._custom_dve(OPC, out=qta[:, 0:ncols], in0=qA[:, 0:ncols],
                          in1=qtb[:, 0:ncols], s0=MAGIC, s1=0.1)
            sexp = smq2[:, 0:nb].unsqueeze(2).broadcast_to([128, nb, 16])
            return e.tensor_tensor(
                out_bf_ap.rearrange("p (b s) -> p b s", s=16),
                qta[:, 0:ncols].rearrange("p (b s) -> p b s", s=16),
                sexp, ALU.mult)

        for t in range(NT):
            buf = t % 2

            @on("sync")
            def _(e, t=t, buf=buf, wv=VV.n, wa=AA.n):
                if t >= 2:
                    e.wait_ge(VV.h, wv)
                    e.wait_ge(AA.h, wa)
                e.dma_start(out=xt_sb[buf][:],
                            in_=x_in[t * 128:(t + 1) * 128, :]).then_inc(LD.h, 16)
            LD.n += 16
            ld_x = LD.n

            @on("scalar")
            def _(e, t=t, buf=buf, wl=ld_x, wv=VV.n):
                e.wait_ge(LD.h, wl)
                if t >= 1:
                    e.wait_ge(VV.h, wv)
                e.activation(sqs_sb[:], xt_sb[buf][:], AF.Square
                             ).then_inc(AA.h, 1)
            AA.n += 1
            aa_sq = AA.n

            RD.n += 3

            @on("vector")
            def _(e, buf=buf, wl=ld_x, wa=aa_sq, rd=(RD.n - 2, RD.n - 1, RD.n)):
                e.wait_ge(LD.h, wl)
                e.tensor_reduce(sm[:, 0:1], xt_sb[buf][:], AX.X, ALU.add
                                ).then_inc(RD.h, 1)
                e.wait_ge(RD.h, rd[0])
                e.tensor_scalar_mul(sm[:, 2:3], sm[:, 0:1], 1.0 / 1024.0)
                e.wait_ge(AA.h, wa)
                e.tensor_reduce(sm[:, 12:13], sqs_sb[:], AX.X, ALU.add
                                ).then_inc(RD.h, 1)
                e.wait_ge(RD.h, rd[1])
                e.tensor_tensor(sm[:, 3:4], sm[:, 2:3], sm[:, 2:3], ALU.mult
                                ).then_inc(RD.h, 1)
                e.wait_ge(RD.h, rd[2])
                e.scalar_tensor_tensor(sm[:, 4:5], sm[:, 12:13], 1.0 / 1024.0,
                                       sm[:, 3:4], ALU.mult, ALU.subtract
                                       ).then_inc(VV.h, 1)
            VV.n += 1
            vv_var = VV.n

            @on("scalar")
            def _(e, t=t, wv=vv_var, er=EPS_RD):
                e.wait_ge(VV.h, wv)
                if t == 0:
                    e.wait_ge(RD.h, er)
                e.activation(sm[:, 13:14], sm[:, 4:5], AF.Sqrt,
                             bias=sm[:, 10:11]).then_inc(AA.h, 1)
            AA.n += 1
            aa_sd = AA.n

            RD.n += 4

            @on("vector")
            def _(e, t=t, buf=buf, wa=aa_sd, wt=TT.n,
                  rds=(RD.n - 3, RD.n - 2, RD.n - 1, RD.n)):
                e.wait_ge(AA.h, wa)
                if t >= 1:
                    e.wait_ge(TT.h, wt)
                e.reciprocal(sm[:, 6:7], sm[:, 13:14]).then_inc(RD.h, 1)
                e.wait_ge(RD.h, rds[0])
                e.tensor_scalar(xn_f32[:], xt_sb[buf][:], sm[:, 2:3],
                                sm[:, 6:7], ALU.subtract, ALU.mult)
                e.tensor_copy(xnbf[:, t * 1024:(t + 1) * 1024], xn_f32[:])
                qdq_emit(e, xn_f32[:], 1024, xqt[:],
                         (rds[1], rds[2], rds[3])).then_inc(VV.h, 1)
            VV.n += 1
            vv_xq = VV.n

            if t == 0 and "ln0" in dbg_d:
                @on("sync")
                def _(e, wv=vv_xq):
                    e.wait_ge(VV.h, wv)
                    e.dma_start(out=dbg_d["ln0"][:, 0:1024], in_=xt_sb[0][:]
                                ).then_inc(ST.h, 16)
                    e.dma_start(out=dbg_d["ln0"][:, 1024:2048], in_=xn_f32[:]
                                ).then_inc(ST.h, 16)
                    e.dma_start(out=dbg_d["ln0"][:, 2048:2064], in_=sm[:]
                                ).then_inc(ST.h, 16)
                ST.n += 48

            for grp in range(2):
                bank = grp

                @on("tensor")
                def _(e, t=t, grp=grp, bank=bank, wv=vv_xq, wl=ld_consts,
                      bg=bank_guard[bank]):
                    pe_wait_bank(e, bg)
                    e.wait_ge(VV.h, wv)
                    if t == 0 and grp == 0:
                        e.wait_ge(LD.h, wl)
                    for j in range(4):
                        ht = grp * 4 + j
                        ii = e.transpose(PSB[bank][:, j * 128:(j + 1) * 128],
                                         xqt[:, ht * 128:(ht + 1) * 128],
                                         ident_sb[:])
                        if j == 3:
                            ii.then_inc(TT.h, 1)
                TT.n += 1
                tt_g = TT.n

                @on("scalar")
                def _(e, t=t, grp=grp, bank=bank, wt=tt_g):
                    e.wait_ge(TT.h, wt)
                    e.copy(xq_fm[:, t * 1024 + grp * 512:
                                 t * 1024 + grp * 512 + 512],
                           PSB[bank][:, 0:512]).then_inc(AA.h, 1)
                AA.n += 1
                guard_bank(bank, AA)

        vv_p1 = VV.n
        aa_p1 = AA.n

        @on("sync")
        def _(e, wv=vv_p1):
            e.wait_ge(VV.h, wv)
            e.dma_start(out=agx_src.rearrange("(t p) c -> p t c", p=128),
                        in_=xnbf[:].rearrange("p (t c) -> p t c", c=1024)
                        ).then_inc(LD.h, 16)
        LD.n += 16
        ld_agx = LD.n

        @on("sync")
        def _(e, wa=aa_p1):
            e.wait_ge(AA.h, wa)
            e.dma_start(out=xq_d, in_=xq_fm[:]).then_inc(LD.h, 16)
        LD.n += 16
        ld_xqd = LD.n

        if "xnbf" in dbg_d:
            @on("sync")
            def _(e, wv=vv_p1):
                e.wait_ge(VV.h, wv)
                e.dma_start(out=dbg_d["xnbf"], in_=xnbf[:]).then_inc(ST.h, 16)
            ST.n += 16
        if "xq" in dbg_d:
            @on("sync")
            def _(e, wa=aa_p1):
                e.wait_ge(AA.h, wa)
                e.dma_start(out=dbg_d["xq"], in_=xq_fm[:]).then_inc(ST.h, 16)
            ST.n += 16

        @on("gpsimd")
        def _(e, wl=ld_agx):
            e.wait_ge(LD.h, wl)
            e.collective_compute("AllGather", ALU.bypass,
                                 replica_groups=[list(range(NC))],
                                 ins=[agx_src.opt()], outs=[agx_dst.opt()]
                                 ).then_inc(CX.h, 1)
        CX.n += 1

        def wprep(src_in, ntile, ncols, dst_sb, dst_off, ld0=0):
            for it in range(ntile):
                @on("sync")
                def _(e, it=it, wv=VV.n):
                    e.wait_ge(VV.h, wv)
                    e.dma_start(out=xt_sb[0][:, 0:ncols],
                                in_=src_in[it * 128:(it + 1) * 128, :]
                                ).then_inc(LD.h, 16)
                LD.n += 16
                ld_w = LD.n

                RD.n += 3

                @on("vector")
                def _(e, it=it, ncols=ncols, wl=ld_w, wt=TT.n,
                      rd=(RD.n - 2, RD.n - 1, RD.n)):
                    e.wait_ge(LD.h, wl)
                    e.wait_ge(TT.h, wt)
                    qdq_emit(e, xt_sb[0][:, 0:ncols], ncols, wqt[:, 0:ncols],
                             rd).then_inc(VV.h, 1)
                VV.n += 1
                vv_w = VV.n

                nsub = ncols // 128
                for grp in range((nsub + 3) // 4):
                    bank = 2 + (grp % 2)

                    @on("tensor")
                    def _(e, it=it, grp=grp, bank=bank, wv=vv_w, nsub=nsub, bg=bank_guard[bank]):
                        pe_wait_bank(e, bg)
                        e.wait_ge(VV.h, wv)
                        lo = grp * 4
                        hi = min(lo + 4, nsub)
                        for j, s in enumerate(range(lo, hi)):
                            ii = e.transpose(PSB[bank][:, j * 128:(j + 1) * 128],
                                             wqt[:, s * 128:(s + 1) * 128],
                                             ident_sb[:])
                            if s == hi - 1:
                                ii.then_inc(TT.h, 1)
                    TT.n += 1
                    tt_w = TT.n

                    @on("scalar")
                    def _(e, it=it, grp=grp, bank=bank, wt=tt_w, nsub=nsub,
                          ld0=ld0):
                        e.wait_ge(TT.h, wt)
                        if it == 0 and grp == 0 and ld0:
                            e.wait_ge(LD.h, ld0)
                        lo = grp * 4
                        hi = min(lo + 4, nsub)
                        for j, s in enumerate(range(lo, hi)):
                            ii = e.copy(dst_sb[:, dst_off(it, s):
                                               dst_off(it, s) + 128],
                                        PSB[bank][:, j * 128:(j + 1) * 128])
                            if s == hi - 1:
                                ii.then_inc(AA.h, 1)
                    AA.n += 1
                    guard_bank(bank, AA)

        def stage_and_collect(sb_src, nblk, blkcols, dst_dram, dst_dst, csem):
            aa_w = AA.n

            @on("sync")
            def _(e, wa=aa_w):
                e.wait_ge(AA.h, wa)
                dst = bass.AP(tensor=dst_dram.tensor, offset=0,
                              ap=[[blkcols, 128], [128 * blkcols, nblk],
                                  [1, blkcols]])
                e.dma_start(out=dst,
                            in_=sb_src[:].rearrange("p (h c) -> p h c",
                                                    c=blkcols)
                            ).then_inc(LD.h, 16)
            LD.n += 16
            ld_s = LD.n

            @on("gpsimd")
            def _(e, wl=ld_s, dst_dram=dst_dram, dst_dst=dst_dst, csem=csem):
                e.wait_ge(LD.h, wl)
                e.collective_compute("AllGather", ALU.bypass,
                                     replica_groups=[list(range(NC))],
                                     ins=[dst_dram.opt()], outs=[dst_dst.opt()]
                                     ).then_inc(csem.h, 1)
            csem.n += 1

        wprep(wg_in, 3, 1024, gT_sb, lambda it, s: s * WSH + it * 128)
        stage_and_collect(gT_sb, 8, WSH, agg_src, agg_dst, CG)
        wprep(wu_in, 3, 1024, gT_sb, lambda it, s: s * WSH + it * 128,
              ld0=LD.n)
        stage_and_collect(gT_sb, 8, WSH, agu_src, agu_dst, CU)
        wprep(wd_in, 8, WSH, dT_sb, lambda it, s: s * 1024 + it * 128)
        stage_and_collect(dT_sb, 3, 1024, agd_src, agd_dst, CD)

        p1.close()

        p2 = ExitStack()
        lamr = sbuf(p2, "lamr", [128, 32 * 129], F32)
        lami = sbuf(p2, "lami", [128, 32 * 129], F32)
        e2buf = sbuf(p2, "e2buf", [128, 32 * 129], F32)
        e2bf = sbuf(p2, "e2bf", [128, 32 * 129], BF16)
        lam2bf = sbuf(p2, "lam2bf", [128, 128 * 32], BF16)
        args_sb = sbuf(p2, "args_sb", [128, 32 * 129], F32)
        argk_sb = sbuf(p2, "argk_sb", [128, 32 * 129], F32)
        dramp_sb = sbuf(p2, "dramp_sb", [128, 129], F32)
        logdt_sb = sbuf(p2, "logdt_sb", [128, 1], F32)
        dcol_sb = sbuf(p2, "dcol_sb", [128, 1], F32)
        cre_sb = sbuf(p2, "cre_sb", [128, 32], F32)
        cim_sb = sbuf(p2, "cim_sb", [128, 32], F32)
        wlr_sb = sbuf(p2, "wlr_sb", [128, 32], F32)
        wim_sb = sbuf(p2, "wim_sb", [128, 32], F32)
        s32 = sbuf(p2, "s32", [128, 32 * 16], F32)
        k_sb = sbuf(p2, "k_sb", [128, 129], F32)
        kext_sb = sbuf(p2, "kext_sb", [128, 256], BF16)

        @on("sync")
        def _(e):
            e.dma_start(out=dramp_sb[:],
                        in_=bass.AP(tensor=dramp_in.tensor, offset=0,
                                    ap=[[0, 128], [1, 129]])).then_inc(LD.h, 16)
            e.dma_start(out=logdt_sb[:], in_=logdt_in).then_inc(LD.h, 16)
            e.dma_start(out=dcol_sb[:], in_=dcol_in).then_inc(LD.h, 16)
            e.dma_start(out=cre_sb[:], in_=cre_in).then_inc(LD.h, 16)
            e.dma_start(out=cim_sb[:], in_=cim_in).then_inc(LD.h, 16)
            e.dma_start(out=wlr_sb[:],
                        in_=bass.AP(tensor=wlr_in.tensor, offset=0,
                                    ap=[[0, 128], [1, 32]])).then_inc(LD.h, 16)
            e.dma_start(out=wim_sb[:],
                        in_=bass.AP(tensor=wim_in.tensor, offset=0,
                                    ap=[[0, 128], [1, 32]])).then_inc(LD.h, 16)
        LD.n += 16 * 7
        ld_p2 = LD.n

        def sc(i):
            return s32[:, i * 32:(i + 1) * 32]

        @on("scalar")
        def _(e, wl=ld_p2):
            e.wait_ge(LD.h, wl)
            e.activation(sm[:, 9:10], logdt_sb[:], AF.Exp)
            e.activation(sc(1), wlr_sb[:], AF.Exp).then_inc(AA.h, 1)
        AA.n += 1
        aa_dt = AA.n

        RD.n += 1
        rd_sc3 = RD.n

        @on("vector")
        def _(e, wa=aa_dt, rd3=rd_sc3):
            e.wait_ge(AA.h, wa)
            e.tensor_scalar(sc(2), sc(1), sm[:, 9:10], -1.0, ALU.mult, ALU.mult)
            e.tensor_scalar_mul(sc(3), wim_sb[:], sm[:, 9:10]).then_inc(RD.h, 1)
            e.wait_ge(RD.h, rd3)
            e.tensor_tensor(args_sb[:].rearrange("p (n d) -> p n d", d=129),
                            sc(3).unsqueeze(2).broadcast_to([128, 32, 129]),
                            dramp_sb[:].unsqueeze(1).broadcast_to([128, 32, 129]),
                            ALU.mult)
            e.tensor_tensor(argk_sb[:].rearrange("p (n d) -> p n d", d=129),
                            sc(2).unsqueeze(2).broadcast_to([128, 32, 129]),
                            dramp_sb[:].unsqueeze(1).broadcast_to([128, 32, 129]),
                            ALU.mult).then_inc(VV.h, 1)
        VV.n += 1
        vv_args = VV.n

        @on("scalar")
        def _(e, wv=vv_args):
            e.wait_ge(VV.h, wv)
            e.activation(lamr[:], argk_sb[:], AF.Exp).then_inc(AA.h, 1)
        AA.n += 1
        aa_mag = AA.n

        @on("vector")
        def _(e, wv=vv_args, wa=aa_mag):
            e.wait_ge(AA.h, wa)
            e._custom_dve(OPR, out=e2buf[:], in0=args_sb[:], s0=MAGIC,
                          s1=INV_2PI)
            e.cody_waite_cascade(argk_sb[:], args_sb[:], e2buf[:],
                                 CW1, CW2, CW3).then_inc(VV.h, 1)
        VV.n += 1
        vv_cw = VV.n

        @on("scalar")
        def _(e, wv=vv_cw):
            e.wait_ge(VV.h, wv)
            e.activation(args_sb[:], argk_sb[:], AF.Sin).then_inc(AA.h, 1)
        AA.n += 1
        aa_sin = AA.n

        @on("vector")
        def _(e, wa=aa_sin):
            e.wait_ge(AA.h, wa)
            e.add_range_wrap(argk_sb[:], argk_sb[:], HALF_PI, PI_F, TWO_PI
                             ).then_inc(VV.h, 1)
        VV.n += 1
        vv_wrap = VV.n

        @on("scalar")
        def _(e, wv=vv_wrap):
            e.wait_ge(VV.h, wv)
            e.activation(argk_sb[:], argk_sb[:], AF.Sin).then_inc(AA.h, 1)
        AA.n += 1
        aa_cos = AA.n

        @on("vector")
        def _(e, wa=aa_cos):
            e.wait_ge(AA.h, wa)
            e.tensor_tensor(lami[:], lamr[:], args_sb[:], ALU.mult)
            e.tensor_tensor(lamr[:], lamr[:], argk_sb[:], ALU.mult
                            ).then_inc(VV.h, 1)
        VV.n += 1
        vv_lam = VV.n

        RD.n += 9
        rdp2_v = RD.n

        @on("vector")
        def _(e, wv=vv_lam, rdp2=rdp2_v):
            rdc = [rdp2 - 8]

            def g(ii):
                ii.then_inc(RD.h, 1)
                e.wait_ge(RD.h, rdc[0])
                rdc[0] += 1
            lam1r = bass.AP(tensor=lamr[:].tensor, offset=1,
                            ap=[[32 * 129, 128], [129, 32]])
            lam1i = bass.AP(tensor=lami[:].tensor, offset=1,
                            ap=[[32 * 129, 128], [129, 32]])
            e.tensor_scalar_add(sc(6), lam1r, -1.0)
            e.tensor_copy(sc(7), lam1i)
            e.tensor_tensor(sc(4), sc(1), sc(1), ALU.mult)
            g(e.tensor_tensor(sc(8), wim_sb[:], wim_sb[:], ALU.mult))
            g(e.tensor_tensor(sc(4), sc(4), sc(8), ALU.add))
            g(e.reciprocal(sc(5), sc(4)))
            e.tensor_tensor(sc(8), sc(6), sc(1), ALU.mult)
            g(e.tensor_tensor(sc(9), sc(7), wim_sb[:], ALU.mult))
            g(e.tensor_tensor(sc(10), sc(9), sc(8), ALU.subtract))
            e.tensor_tensor(sc(10), sc(10), sc(5), ALU.mult)
            e.tensor_tensor(sc(8), sc(7), sc(1), ALU.mult)
            g(e.tensor_tensor(sc(9), sc(6), wim_sb[:], ALU.mult))
            g(e.scalar_tensor_tensor(sc(11), sc(8), -1.0, sc(9), ALU.mult,
                                     ALU.subtract))
            e.tensor_tensor(sc(11), sc(11), sc(5), ALU.mult)
            e.tensor_tensor(sc(12), cre_sb[:], sc(10), ALU.mult)
            g(e.tensor_tensor(sc(8), cim_sb[:], sc(11), ALU.mult))
            e.tensor_tensor(sc(12), sc(12), sc(8), ALU.subtract)
            e.tensor_tensor(sc(13), cre_sb[:], sc(11), ALU.mult)
            g(e.tensor_tensor(sc(8), cim_sb[:], sc(10), ALU.mult))
            e.tensor_tensor(sc(13), sc(13), sc(8), ALU.add)
            e.tensor_scalar(sc(14), sc(12), 2.0, None, ALU.mult)
            e.tensor_scalar(sc(15), sc(13), 2.0, None, ALU.mult
                            ).then_inc(VV.h, 1)
        VV.n += 1
        vv_cwv = VV.n

        RD.n += 1
        RD_K = RD.n

        @on("vector")
        def _(e, wl=ld_consts, rdk=RD_K):
            lr3 = lamr[:].rearrange("p (n d) -> p n d", d=129)
            li3 = lami[:].rearrange("p (n d) -> p n d", d=129)
            e.tensor_tensor(e2buf[:].rearrange("p (n d) -> p n d", d=129), lr3,
                            sc(14).unsqueeze(2).broadcast_to([128, 32, 129]),
                            ALU.mult)
            e.tensor_tensor(args_sb[:].rearrange("p (n d) -> p n d", d=129),
                            li3,
                            sc(15).unsqueeze(2).broadcast_to([128, 32, 129]),
                            ALU.mult)
            e.tensor_tensor(e2buf[:], e2buf[:], args_sb[:], ALU.subtract)
            kv = bass.AP(tensor=e2buf[:].tensor, offset=0,
                         ap=[[32 * 129, 128], [1, 129], [129, 32]])
            e.tensor_reduce(k_sb[:].unsqueeze(2), kv, AX.X, ALU.add
                            ).then_inc(RD.h, 1)
            e.wait_ge(RD.h, rdk)
            e.tensor_tensor(k_sb[:, 0:1], k_sb[:, 0:1], dcol_sb[:], ALU.add)
            e.memset(kext_sb[:], 0.0)
            e.tensor_copy(kext_sb[:, 127:255], k_sb[:, 0:128])
            e.tensor_copy(e2bf[:], e2buf[:]).then_inc(VV.h, 1)
        VV.n += 1
        vv_e2r = VV.n

        @on("sync")
        def _(e, wv=vv_e2r):
            e.wait_ge(VV.h, wv)
            e.dma_start(out=e2r_d, in_=e2bf[:]).then_inc(LD.h, 16)
            e.dma_start(out=kext_d, in_=kext_sb[:]).then_inc(LD.h, 16)
        LD.n += 32
        ld_e2r = LD.n

        if "kext" in dbg_d:
            @on("sync")
            def _(e, wv=vv_e2r):
                e.wait_ge(VV.h, wv)
                e.dma_start(out=dbg_d["kext"], in_=kext_sb[:]).then_inc(ST.h, 16)
            ST.n += 16

        @on("vector")
        def _(e, wl=ld_e2r):
            e.wait_ge(LD.h, wl)
            e.tensor_tensor(e2buf[:].rearrange("p (n d) -> p n d", d=129),
                            lami[:].rearrange("p (n d) -> p n d", d=129),
                            sc(14).unsqueeze(2).broadcast_to([128, 32, 129]),
                            ALU.mult)
            e.tensor_tensor(args_sb[:].rearrange("p (n d) -> p n d", d=129),
                            lamr[:].rearrange("p (n d) -> p n d", d=129),
                            sc(15).unsqueeze(2).broadcast_to([128, 32, 129]),
                            ALU.mult)
            e.tensor_tensor(e2buf[:], e2buf[:], args_sb[:], ALU.add)
            e.tensor_scalar_mul(e2buf[:], e2buf[:], -1.0)
            e.tensor_copy(e2bf[:], e2buf[:]).then_inc(VV.h, 1)
        VV.n += 1
        vv_e2i = VV.n

        @on("sync")
        def _(e, wv=vv_e2i):
            e.wait_ge(VV.h, wv)
            e.dma_start(out=e2i_d, in_=e2bf[:]).then_inc(LD.h, 16)
        LD.n += 16

        @on("vector")
        def _(e):
            lrd = bass.AP(tensor=lamr[:].tensor, offset=0,
                          ap=[[32 * 129, 128], [1, 128], [129, 32]])
            e.tensor_copy(lam2bf[:].rearrange("p (d n) -> p d n", n=32), lrd
                          ).then_inc(VV.h, 1)
        VV.n += 1
        vv_l2r = VV.n

        @on("sync")
        def _(e, wv=vv_l2r):
            e.wait_ge(VV.h, wv)
            e.dma_start(out=lam2r_d, in_=lam2bf[:]).then_inc(LD.h, 16)
        LD.n += 16
        ld_l2r = LD.n

        @on("vector")
        def _(e, wl=ld_l2r):
            e.wait_ge(LD.h, wl)
            lid = bass.AP(tensor=lami[:].tensor, offset=0,
                          ap=[[32 * 129, 128], [1, 128], [129, 32]])
            e.tensor_copy(lam2bf[:].rearrange("p (d n) -> p d n", n=32), lid)
            lTr = bass.AP(tensor=lamr[:].tensor, offset=128,
                          ap=[[32 * 129, 128], [129, 32]])
            lTi = bass.AP(tensor=lami[:].tensor, offset=128,
                          ap=[[32 * 129, 128], [129, 32]])
            e.tensor_copy(mtr_sb[:].rearrange("p (b n) -> p b n", n=32),
                          lTr.unsqueeze(1).broadcast_to([128, 8, 32]))
            e.tensor_copy(mti_sb[:].rearrange("p (b h n) -> p b h n", h=2, n=32),
                          lTi.unsqueeze(1).unsqueeze(1).broadcast_to(
                              [128, 4, 2, 32]))
            neg = bass.AP(tensor=mti_sb[:].tensor, offset=0,
                          ap=[[256, 128], [64, 4], [1, 32]])
            e.tensor_scalar_mul(neg, neg, -1.0).then_inc(VV.h, 1)
        VV.n += 1
        vv_l2i = VV.n

        @on("sync")
        def _(e, wv=vv_l2i):
            e.wait_ge(VV.h, wv)
            e.dma_start(out=lam2i_d, in_=lam2bf[:]).then_inc(LD.h, 16)
        LD.n += 16
        ld_lam = LD.n
        p2.close()

        p3 = ExitStack()
        ubuf = [sbuf(p3, f"ubuf{i}", [128, 16 * 128], BF16) for i in range(2)]
        b64 = sbuf(p3, "b64", [64, 128 * 128], BF16)
        a2b = sbuf(p3, "a2b", [128, NB * 32 * 64], BF16)
        s2b = sbuf(p3, "s2b", [128, NB * 32 * 64], BF16)
        br = [sbuf(p3, f"br{i}", [128, 256], F32) for i in range(2)]
        bsw = sbuf(p3, "bsw", [128, 256], F32)
        sc1 = sbuf(p3, "sc1", [128, 256], F32)
        sc2 = sbuf(p3, "sc2", [128, 256], F32)
        tgb = [sbuf(p3, f"tgb{i}", [128, 8 * 128], BF16) for i in range(2)]
        ggb = [sbuf(p3, f"ggb{i}", [64, 8 * 128], BF16) for i in range(2)]
        pbuf = [sbuf(p3, f"pbuf{i}", [128, 8 * 64], BF16) for i in range(2)]
        gyb = [sbuf(p3, f"gyb{i}", [128, 1024], BF16) for i in range(2)]
        gy = sbuf(p3, "gy", [128, 128 * 128], BF16)
        urev = ug_buf

        for kb in range(8):
            buf = kb % 2

            @on("sync")
            def _(e, kb=kb, buf=buf, wc=CX.n, wt=TT.n):
                if kb == 0:
                    e.wait_ge(CX.h, wc)
                if kb >= 2:
                    e.wait_ge(TT.h, wt)
                pid128, _ = get_pid(e)
                src = bass.AP(tensor=agx_dst.tensor,
                              offset=pid128 + kb * 16 * 128 * H,
                              ap=[[H, 128], [128 * H, 16], [1, 128]])
                e.dma_start(out=ubuf[buf][:].rearrange("p (k h) -> p k h", h=128),
                            in_=src).then_inc(LD.h, 16)
            LD.n += 16
            ld_u = LD.n

            for g4 in range(4):
                bank = 4 + (g4 % 2)

                @on("tensor")
                def _(e, g4=g4, buf=buf, bank=bank, wl=ld_u, bg=bank_guard[bank]):
                    pe_wait_bank(e, bg)
                    e.wait_ge(LD.h, wl)
                    for j in range(4):
                        k = g4 * 4 + j
                        ii = e.matmul(PS[bank][:, j * 128:(j + 1) * 128],
                                      jrev_sb[:],
                                      ubuf[buf][:, k * 128:(k + 1) * 128],
                                      start=True, stop=True)
                        if j == 3:
                            ii.then_inc(TT.h, 1)
                TT.n += 1
                tt_j = TT.n

                @on("scalar")
                def _(e, kb=kb, g4=g4, bank=bank, wt=tt_j):
                    e.wait_ge(TT.h, wt)
                    e.copy(urev[:, kb * 2048 + g4 * 512:
                                kb * 2048 + g4 * 512 + 512],
                           PS[bank][:, 0:512]).then_inc(AA.h, 1)
                AA.n += 1
                guard_bank(bank, AA)
        aa_urev = AA.n

        if "urev" in dbg_d:
            @on("sync")
            def _(e, wa=aa_urev):
                e.wait_ge(AA.h, wa)
                e.dma_start(out=dbg_d["urev"], in_=urev[:]).then_inc(ST.h, 16)
            ST.n += 16

        for hb in range(16):
            buf = hb % 2

            @on("sync")
            def _(e, hb=hb, buf=buf, wl=ld_lam, wt=TT.n):
                if hb == 0:
                    e.wait_ge(LD.h, wl)
                if hb >= 2:
                    e.wait_ge(TT.h, wt)
                h0 = hb * 8
                srcr = bass.AP(tensor=lam2r_d.tensor, offset=h0 * 4096,
                               ap=[[32, 128], [4096, 8], [1, 32]])
                srci = bass.AP(tensor=lam2i_d.tensor, offset=h0 * 4096,
                               ap=[[32, 128], [4096, 8], [1, 32]])
                pb = pbuf[buf][:].rearrange("p (j h q) -> p j h q", h=2, q=32)
                e.dma_start(out=pb[:, :, 0, :], in_=srcr).then_inc(LD.h, 16)
                e.dma_start(out=pb[:, :, 1, :], in_=srci).then_inc(LD.h, 16)
            LD.n += 32
            ld_p = LD.n

            for half in range(2):
                bank = 6 + (half % 2)

                @on("tensor")
                def _(e, hb=hb, half=half, buf=buf, bank=bank, wl=ld_p,
                      wa=aa_urev, bg=bank_guard[bank]):
                    pe_wait_bank(e, bg)
                    e.wait_ge(LD.h, wl)
                    if hb == 0 and half == 0:
                        e.wait_ge(AA.h, wa)
                    for j in range(4):
                        hh = half * 4 + j
                        h = hb * 8 + hh
                        rhs = bass.AP(tensor=urev[:].tensor, offset=h,
                                      ap=[[16384, 128], [128, 128]])
                        ii = e.matmul(PS[bank][0:64, j * 128:(j + 1) * 128],
                                      pbuf[buf][:, hh * 64:(hh + 1) * 64],
                                      rhs, start=True, stop=True)
                        if j == 3:
                            ii.then_inc(TT.h, 1)
                TT.n += 1
                tt_p = TT.n

                @on("scalar")
                def _(e, hb=hb, half=half, bank=bank, wt=tt_p):
                    e.wait_ge(TT.h, wt)
                    h0 = hb * 8 + half * 4
                    e.copy(b64[:, h0 * 128:(h0 + 4) * 128],
                           PS[bank][0:64, 0:512]).then_inc(AA.h, 1)
                AA.n += 1
                guard_bank(bank, AA)
        aa_A = AA.n

        for g8 in range(16):
            bank = g8 % 2

            @on("tensor")
            def _(e, g8=g8, bank=bank, wa=aa_A, bg=bank_guard[bank]):
                pe_wait_bank(e, bg)
                if g8 == 0:
                    e.wait_ge(AA.h, wa)
                for j in range(8):
                    k = g8 * 8 + j
                    src = bass.AP(tensor=b64[:].tensor, offset=k,
                                  ap=[[16384, 64], [128, 128]])
                    ii = e.transpose(PSB[bank][:, j * 64:(j + 1) * 64],
                                     src, ident_sb[0:64, 0:64])
                    if j == 7:
                        ii.then_inc(TT.h, 1)
            TT.n += 1
            tt_at = TT.n

            @on("scalar")
            def _(e, g8=g8, bank=bank, wt=tt_at):
                e.wait_ge(TT.h, wt)
                k0 = g8 * 8
                b = k0 // 32
                cc0 = k0 % 32
                e.copy(a2b[:, b * 2048 + cc0 * 64: b * 2048 + cc0 * 64 + 512],
                       PSB[bank][:, 0:512]).then_inc(AA.h, 1)
            AA.n += 1
            guard_bank(bank, AA)
        aa_A2 = AA.n

        @on("vector")
        def _(e, wa=aa_A2, wv=vv_l2i):
            e.wait_ge(AA.h, wa)
            e.wait_ge(VV.h, wv)
            a3 = a2b[:].rearrange("p (b c q) -> p b c q", c=32, q=64)
            s3 = s2b[:].rearrange("p (b c q) -> p b c q", c=32, q=64)
            for b in range(NB):
                e.memset(s2b[:, b * 2048:b * 2048 + 64], 0.0)
            e.tensor_copy(br[1][:].rearrange("p (b q) -> p b q", q=64),
                          a3[:, :, 0, :])
            e.tensor_copy(s3[:, :, 1, :],
                          br[1][:].rearrange("p (b q) -> p b q", q=64))
            for cc in range(1, 32):
                cur = br[cc % 2]
                nxt = br[(cc + 1) % 2]
                bv = cur[:].rearrange("p (b h n) -> p b h n", h=2, n=32)
                sv = bsw[:].rearrange("p (b h n) -> p b h n", h=2, n=32)
                e.tensor_copy(sv[:, :, 0, :], bv[:, :, 1, :])
                e.tensor_copy(sv[:, :, 1, :], bv[:, :, 0, :])
                e.tensor_tensor(sc1[:], cur[:], mtr_sb[:], ALU.mult)
                e.tensor_tensor(sc2[:], bsw[:], mti_sb[:], ALU.mult)
                e.tensor_tensor(sc1[:], sc1[:], sc2[:], ALU.add)
                e.tensor_tensor(nxt[:].rearrange("p (b q) -> p b q", q=64),
                                sc1[:].rearrange("p (b q) -> p b q", q=64),
                                a3[:, :, cc, :], ALU.add)
                if cc < 31:
                    e.tensor_copy(s3[:, :, cc + 1, :],
                                  nxt[:].rearrange("p (b q) -> p b q", q=64))
            e.engine_nop().then_inc(VV.h, 1)
        VV.n += 1
        vv_scan = VV.n

        if "s2b" in dbg_d:
            @on("sync")
            def _(e, wv=vv_scan):
                e.wait_ge(VV.h, wv)
                e.dma_start(out=dbg_d["s2b"], in_=s2b[:]).then_inc(ST.h, 16)
            ST.n += 16

        for g2 in range(64):
            bank = 2 + (g2 % 2)

            @on("tensor")
            def _(e, g2=g2, bank=bank, wv=vv_scan, bg=bank_guard[bank]):
                pe_wait_bank(e, bg)
                if g2 == 0:
                    e.wait_ge(VV.h, wv)
                for j in range(2):
                    k = g2 * 2 + j
                    b = k // 32
                    cc = k % 32
                    ii = e.transpose(PSB[bank][0:64, j * 128:(j + 1) * 128],
                                     s2b[:, b * 2048 + cc * 64:
                                         b * 2048 + cc * 64 + 64],
                                     ident_sb[:])
                    if j == 1:
                        ii.then_inc(TT.h, 1)
            TT.n += 1
            tt_st = TT.n

            @on("scalar")
            def _(e, g2=g2, bank=bank, wt=tt_st):
                e.wait_ge(TT.h, wt)
                k0 = g2 * 2
                e.copy(b64[:, k0 * 128:(k0 + 2) * 128],
                       PSB[bank][0:64, 0:256]).then_inc(AA.h, 1)
            AA.n += 1
            guard_bank(bank, AA)
        aa_ST = AA.n

        for hb in range(16):
            buf = hb % 2

            @on("sync")
            def _(e, hb=hb, buf=buf, wt=TT.n, wk=ld_e2r):
                if hb == 0:
                    e.wait_ge(LD.h, wk)
                if hb >= 2:
                    e.wait_ge(TT.h, wt)
                h0 = hb * 8
                tsrc = bass.AP(tensor=kext_d.tensor, offset=h0 * 256,
                               ap=[[1, 128], [256, 8], [1, 128]])
                e.dma_start(out=tgb[buf][:].rearrange("p (j t) -> p j t", t=128),
                            in_=tsrc).then_inc(LD.h, 16)
                gr = bass.AP(tensor=e2r_d.tensor, offset=h0 * 4128 + 1,
                             ap=[[129, 32], [4128, 8], [1, 128]])
                gi = bass.AP(tensor=e2i_d.tensor, offset=h0 * 4128 + 1,
                             ap=[[129, 32], [4128, 8], [1, 128]])
                e.dma_start(out=ggb[buf][0:32, :].rearrange(
                    "p (j t) -> p j t", t=128), in_=gr).then_inc(LD.h, 16)
                e.dma_start(out=ggb[buf][32:64, :].rearrange(
                    "p (j t) -> p j t", t=128), in_=gi).then_inc(LD.h, 16)
            LD.n += 48
            ld_tg = LD.n

            for half in range(2):
                bank = 4 + (half % 2)

                @on("tensor")
                def _(e, hb=hb, half=half, buf=buf, bank=bank, wl=ld_tg,
                      wa=aa_ST, bg=bank_guard[bank]):
                    pe_wait_bank(e, bg)
                    e.wait_ge(LD.h, wl)
                    if hb == 0 and half == 0:
                        e.wait_ge(AA.h, wa)
                    for j in range(4):
                        hh = half * 4 + j
                        h = hb * 8 + hh
                        rhsu = bass.AP(tensor=urev[:].tensor, offset=h,
                                       ap=[[16384, 128], [128, 128]])
                        rhss = bass.AP(tensor=b64[:].tensor, offset=h,
                                       ap=[[16384, 64], [128, 128]])
                        e.matmul(PS[bank][:, j * 128:(j + 1) * 128],
                                 tgb[buf][:, hh * 128:(hh + 1) * 128],
                                 rhsu, start=True, stop=False)
                        ii = e.matmul(PS[bank][:, j * 128:(j + 1) * 128],
                                      ggb[buf][0:64, hh * 128:(hh + 1) * 128],
                                      rhss, start=False, stop=True)
                        if j == 3:
                            ii.then_inc(TT.h, 1)
                TT.n += 1
                tt_y = TT.n

                @on("scalar")
                def _(e, hb=hb, half=half, bank=bank, wt=tt_y):
                    e.wait_ge(TT.h, wt)
                    h0 = hb * 8 + half * 4
                    e.activation(gy[:, h0 * 128:(h0 + 4) * 128],
                                 PS[bank][:, 0:512], AF.Gelu).then_inc(AA.h, 1)
                AA.n += 1
                guard_bank(bank, AA)
        aa_gy = AA.n

        if "gy" in dbg_d:
            @on("sync")
            def _(e, wa=aa_gy):
                e.wait_ge(AA.h, wa)
                e.dma_start(out=dbg_d["gy"], in_=gy[:]).then_inc(ST.h, 16)
            ST.n += 16

        for g8 in range(16):
            bank = g8 % 2
            bbuf = g8 % 2

            @on("tensor")
            def _(e, g8=g8, bank=bank, wa=aa_gy, wl=GY.n, bg=bank_guard[bank]):
                pe_wait_bank(e, bg)
                if g8 == 0:
                    e.wait_ge(AA.h, wa)
                if g8 >= 2:
                    e.wait_ge(GY.h, wl)
                for j in range(8):
                    k = g8 * 8 + j
                    src = bass.AP(tensor=gy[:].tensor, offset=k,
                                  ap=[[16384, 128], [128, 128]])
                    ii = e.transpose(PSB[bank][:, j * 128:(j + 1) * 128],
                                     src, ident_sb[:])
                    if j == 7:
                        ii.then_inc(TT.h, 1)
            TT.n += 1
            tt_gyt = TT.n

            @on("scalar")
            def _(e, g8=g8, bank=bank, bbuf=bbuf, wt=tt_gyt):
                e.wait_ge(TT.h, wt)
                e.copy(gyb[bbuf][:], PSB[bank][:, 0:1024]).then_inc(AA.h, 1)
            AA.n += 1
            aa_gyb = AA.n
            guard_bank(bank, AA)

            @on("sync")
            def _(e, g8=g8, bbuf=bbuf, wa=aa_gyb):
                e.wait_ge(AA.h, wa)
                e.dma_start(out=agy_src[:, g8 * 1024:(g8 + 1) * 1024],
                            in_=gyb[bbuf][:]).then_inc(GY.h, 16)
            GY.n += 16
        ld_agy = GY.n

        @on("gpsimd")
        def _(e, wl=ld_agy):
            e.wait_ge(GY.h, wl)
            e.collective_compute("AllGather", ALU.bypass,
                                 replica_groups=[list(range(NC))],
                                 ins=[agy_src.opt()], outs=[agy_dst.opt()]
                                 ).then_inc(CY.h, 1)
        CY.n += 1
        p3.close()

        p4 = ExitStack()
        g_sb = sbuf(p4, "g_sb", [128, NT * 1024], BF16)
        p4a = ExitStack()
        xqs = sbuf(p4a, "xqs", [128, 4 * 1024], BF16)
        wgu = [sbuf(p4a, f"wgu{i}", [128, 2 * 8 * WSH], BF16) for i in range(2)]
        h_f = sbuf(p4a, "h_f", [128, 4 * I_REAL], F32)
        hq = sbuf(p4a, "hq", [128, 4 * I_REAL], BF16)
        qA4 = sbuf(p4a, "qA4", [128, I_REAL], F32)
        qta4 = sbuf(p4a, "qta4", [128, I_REAL], BF16)
        qtb4 = sbuf(p4a, "qtb4", [128, I_REAL], BF16)
        sgl = sbuf(p4a, "sgl", [128, 4 * 384], F32)
        wdb = [sbuf(p4a, f"wdb{i}", [128, 1024], BF16) for i in range(2)]
        hqT = ug_buf

        def ipk_src_row(gtile):
            if gtile < 18:
                r = gtile // 3
                return r, (gtile - 3 * r) * 128
            r = 6 + (gtile - 18) // 2
            return r, ((gtile - 18) % 2) * 128

        tt_pre_ffn = TT.n
        vv_hq_prev = [0]
        for s_ in range(NST):
            @on("sync")
            def _(e, s_=s_, wl=ld_xqd, wt=(tt_pre_ffn if s_ == 0 else TT.n),
                  wgy=ld_agy, wvp=vv_hq_prev[0]):
                e.wait_ge(TT.h, wt)
                if s_ == 0:
                    e.wait_ge(LD.h, wl)
                    e.wait_ge(GY.h, wgy)
                else:
                    e.wait_ge(VV.h, wvp)
                e.dma_start(out=xqs[:],
                            in_=xq_d[:, s_ * 4096:(s_ + 1) * 4096]
                            ).then_inc(LD.h, 16)
            LD.n += 16
            ld_xqs = LD.n

            for r in range(NC):
                buf = r % 2
                blk = BLK_I[r]

                @on("sync")
                def _(e, s_=s_, r=r, buf=buf, wcg=CG.n, wcu=CU.n, wt=TT.n):
                    if s_ == 0 and r == 0:
                        e.wait_ge(CG.h, wcg)
                        e.wait_ge(CU.h, wcu)
                        e.wait_ge(TT.h, wt)
                    if s_ > 0 or r >= 2:
                        e.wait_ge(TT.h, wt)
                    gsrc = bass.AP(tensor=agg_dst.tensor, offset=r * 1024 * WSH,
                                   ap=[[WSH, 128], [128 * WSH, 8], [1, WSH]])
                    usrc = bass.AP(tensor=agu_dst.tensor, offset=r * 1024 * WSH,
                                   ap=[[WSH, 128], [128 * WSH, 8], [1, WSH]])
                    wv = wgu[buf][:].rearrange("p (w j c) -> p w j c", w=2,
                                               c=WSH)
                    e.dma_start(out=wv[:, 0], in_=gsrc).then_inc(LD.h, 16)
                    e.dma_start(out=wv[:, 1], in_=usrc).then_inc(LD.h, 16)
                LD.n += 32
                ld_wgu = LD.n

                if s_ == 0 and r == 0 and "wgu0" in dbg_d:
                    @on("sync")
                    def _(e, wl=ld_wgu):
                        e.wait_ge(LD.h, wl)
                        e.dma_start(out=dbg_d["wgu0"], in_=wgu[0][:]
                                    ).then_inc(ST.h, 16)
                    ST.n += 16
                if s_ == 0 and r == 0 and "xqs0" in dbg_d:
                    @on("sync")
                    def _(e, wl=ld_xqs):
                        e.wait_ge(LD.h, wl)
                        e.dma_start(out=dbg_d["xqs0"], in_=xqs[:]
                                    ).then_inc(ST.h, 16)
                    ST.n += 16

                for tt_i in range(4):
                    bkg = 2 * tt_i
                    bku = 2 * tt_i + 1

                    @on("tensor")
                    def _(e, r=r, tt_i=tt_i, buf=buf, blk=blk, bkg=bkg,
                          bku=bku, wl=ld_wgu, wlx=ld_xqs,
                          bg1=bank_guard[bkg], bg2=bank_guard[bku]):
                        pe_wait_bank(e, bg1)
                        pe_wait_bank(e, bg2)
                        e.wait_ge(LD.h, wl)
                        if r == 0 and tt_i == 0:
                            e.wait_ge(LD.h, wlx)
                        wv = wgu[buf][:]
                        for kt in range(8):
                            lhs = xqs[:, tt_i * 1024 + kt * 128:
                                      tt_i * 1024 + (kt + 1) * 128]
                            e.matmul(PS[bkg][:, 0:blk], lhs,
                                     wv[:, kt * WSH: kt * WSH + blk],
                                     start=(kt == 0), stop=(kt == 7))
                            ii = e.matmul(PS[bku][:, 0:blk], lhs,
                                          wv[:, 8 * WSH + kt * WSH:
                                             8 * WSH + kt * WSH + blk],
                                          start=(kt == 0), stop=(kt == 7))
                            if kt == 7:
                                ii.then_inc(TT.h, 1)
                    TT.n += 1
                    tt_gu = TT.n

                    @on("scalar")
                    def _(e, tt_i=tt_i, blk=blk, bkg=bkg, wt=tt_gu):
                        e.wait_ge(TT.h, wt)
                        e.activation(sgl[:, tt_i * 384:tt_i * 384 + blk],
                                     PS[bkg][:, 0:blk], AF.Silu
                                     ).then_inc(AA.h, 1)
                    AA.n += 1
                    aa_sg = AA.n
                    guard_bank(bkg, AA)

                    @on("vector")
                    def _(e, r=r, tt_i=tt_i, blk=blk, bku=bku, wa=aa_sg):
                        e.wait_ge(AA.h, wa)
                        e.tensor_tensor(
                            h_f[:, tt_i * I_REAL + BLK_OFF[r]:
                                tt_i * I_REAL + BLK_OFF[r] + blk],
                            sgl[:, tt_i * 384:tt_i * 384 + blk],
                            PS[bku][:, 0:blk], ALU.mult).then_inc(VV.h, 1)
                    VV.n += 1
                    guard_bank(bku, VV)

            for tt_i in range(4):
                RD.n += 3

                @on("vector")
                def _(e, s_=s_, tt_i=tt_i, rd=(RD.n - 2, RD.n - 1, RD.n),
                      wlagy=ld_agy, wst=ST.n):
                    if s_ == 0 and tt_i == 0:
                        e.wait_ge(GY.h, wlagy)
                    if s_ == 1 and tt_i == 0 and "hst0" in dbg_d:
                        e.wait_ge(ST.h, wst)
                    src = h_f[:, tt_i * I_REAL:(tt_i + 1) * I_REAL]
                    s3 = src.rearrange("p (b s) -> p b s", s=16)
                    e.tensor_reduce(smq[:, 0:176].unsqueeze(2), s3, AX.X,
                                    ALU.max, apply_absolute_value=True
                                    ).then_inc(RD.h, 1)
                    e.wait_ge(RD.h, rd[0])
                    e.tensor_scalar(smq2[:, 0:176], smq[:, 0:176], 1.0 / 6.0,
                                    1e-12, ALU.mult, ALU.max).then_inc(RD.h, 1)
                    e.wait_ge(RD.h, rd[1])
                    e.reciprocal(smq3[:, 0:176], smq2[:, 0:176]
                                 ).then_inc(RD.h, 1)
                    e.wait_ge(RD.h, rd[2])
                    rexp = smq3[:, 0:176].unsqueeze(2).broadcast_to(
                        [128, 176, 16])
                    e.tensor_tensor(qA4[:].rearrange("p (b s) -> p b s", s=16),
                                    s3, rexp, ALU.mult)
                    e._custom_dve(OPA, out=qta4[:], in0=qA4[:], s0=MAGIC,
                                  s1=4.0, imm2=0.5)
                    e._custom_dve(OPB, out=qtb4[:], in0=qA4[:], in1=qta4[:],
                                  s0=MAGIC, s1=4.0, imm2=2.0)
                    e._custom_dve(OPC, out=qta4[:], in0=qA4[:], in1=qtb4[:],
                                  s0=MAGIC, s1=0.1)
                    sexp = smq2[:, 0:176].unsqueeze(2).broadcast_to(
                        [128, 176, 16])
                    dst = hq[:, tt_i * I_REAL:(tt_i + 1) * I_REAL]
                    e.tensor_tensor(dst.rearrange("p (b s) -> p b s", s=16),
                                    qta4[:].rearrange("p (b s) -> p b s", s=16),
                                    sexp, ALU.mult).then_inc(VV.h, 1)
                VV.n += 1
            vv_hq = VV.n
            vv_hq_prev[0] = vv_hq

            if "hst0" in dbg_d and s_ == 0:
                @on("sync")
                def _(e, wv=vv_hq):
                    e.wait_ge(VV.h, wv)
                    e.dma_start(out=dbg_d["hst0"], in_=hq[:]).then_inc(ST.h, 16)
                ST.n += 16
            if "hf0" in dbg_d and s_ == 0:
                @on("sync")
                def _(e, wv=vv_hq):
                    e.wait_ge(VV.h, wv)
                    e.dma_start(out=dbg_d["hf0"], in_=h_f[:]).then_inc(ST.h, 16)
                ST.n += 16

            for gt in range(NI):
                for pair in range(2):
                    bank = 4 + ((gt * 2 + pair) % 2)

                    @on("tensor")
                    def _(e, gt=gt, pair=pair, bank=bank, wv=vv_hq, bg=bank_guard[bank]):
                        pe_wait_bank(e, bg)
                        if gt == 0 and pair == 0:
                            e.wait_ge(VV.h, wv)
                        for j in range(2):
                            tt_i = pair * 2 + j
                            ii = e.transpose(
                                PSB[bank][:, j * 128:(j + 1) * 128],
                                hq[:, tt_i * I_REAL + gt * 128:
                                   tt_i * I_REAL + (gt + 1) * 128],
                                ident_sb[:])
                            if j == 1:
                                ii.then_inc(TT.h, 1)
                    TT.n += 1
                    tt_hqt = TT.n

                    @on("scalar")
                    def _(e, gt=gt, pair=pair, bank=bank, wt=tt_hqt):
                        e.wait_ge(TT.h, wt)
                        for j in range(2):
                            tt_i = pair * 2 + j
                            ii = e.copy(hqT[:, (tt_i * NI + gt) * 128:
                                            (tt_i * NI + gt) * 128 + 128],
                                        PSB[bank][:, j * 128:(j + 1) * 128])
                            if j == 1:
                                ii.then_inc(AA.h, 1)
                    AA.n += 1
                    guard_bank(bank, AA)
            aa_hqT = AA.n

            for gt in range(NI):
                wbuf = gt % 2
                r, roff = ipk_src_row(gt)

                @on("sync")
                def _(e, gt=gt, wbuf=wbuf, r=r, roff=roff, wcd=CD.n, wt=TT.n):
                    if gt == 0:
                        e.wait_ge(CD.h, wcd)
                    if gt >= 2:
                        e.wait_ge(TT.h, wt)
                    e.dma_start(out=wdb[wbuf][:],
                                in_=agd_dst[r * WSH + roff:
                                            r * WSH + roff + 128, :]
                                ).then_inc(LD.h, 16)
                LD.n += 16
                ld_wd = LD.n

                @on("tensor")
                def _(e, gt=gt, wbuf=wbuf, wl=ld_wd, wa=aa_hqT,
                      bgs=[bank_guard[b] for b in range(8)]):
                    e.wait_ge(LD.h, wl)
                    if gt == 0:
                        e.wait_ge(AA.h, wa)
                        for bg_ in bgs:
                            pe_wait_bank(e, bg_)
                    for tt_i in range(4):
                        for ch in range(2):
                            bank = 2 * tt_i + ch
                            ii = e.matmul(
                                PS[bank][:, 0:512],
                                hqT[:, (tt_i * NI + gt) * 128:
                                    (tt_i * NI + gt) * 128 + 128],
                                wdb[wbuf][:, ch * 512:(ch + 1) * 512],
                                start=(gt == 0), stop=(gt == NI - 1))
                            if tt_i == 3 and ch == 1:
                                ii.then_inc(TT.h, 1)
                TT.n += 1
            tt_down = TT.n

            for tt_i in range(4):
                @on("scalar")
                def _(e, s_=s_, tt_i=tt_i, wt=tt_down):
                    e.wait_ge(TT.h, wt)
                    t = s_ * 4 + tt_i
                    e.activation(g_sb[:, t * 1024:t * 1024 + 512],
                                 PS[2 * tt_i][:, 0:512], AF.Silu)
                    e.activation(g_sb[:, t * 1024 + 512:(t + 1) * 1024],
                                 PS[2 * tt_i + 1][:, 0:512], AF.Silu
                                 ).then_inc(AA.h, 1)
                AA.n += 1
                guard_bank(2 * tt_i, AA)
                guard_bank(2 * tt_i + 1, AA)
        aa_ffn = AA.n

        if "g" in dbg_d:
            @on("sync")
            def _(e, wa=aa_ffn):
                e.wait_ge(AA.h, wa)
                e.dma_start(out=dbg_d["g"], in_=g_sb[:]).then_inc(ST.h, 16)
            ST.n += 16

        p4a.close()
        woutc = sbuf(p4, "woutc", [128, 2048], BF16)
        woutT = sbuf(p4, "woutT", [128, 8 * 1024], BF16)
        wof = sbuf(p4, "wof", [128, 2048], F32)
        xr_sb = [sbuf(p4, f"xr{i}", [128, 1024], F32) for i in range(2)]
        sg_f = sbuf(p4, "sg_f", [128, 1024], F32)
        out_f = [sbuf(p4, f"of{i}", [128, 1024], F32) for i in range(2)]
        gyret = ug_buf

        for chunk in range(4):
            @on("sync")
            def _(e, chunk=chunk, wa=AA.n):
                if chunk >= 1:
                    e.wait_ge(AA.h, wa)
                e.dma_start(out=wof[:].rearrange("p (a c) -> p a c", a=2),
                            in_=wout_in.rearrange("(a p) c -> p a c", p=128)
                            [:, chunk * 2:chunk * 2 + 2, :]).then_inc(LD.h, 16)
            LD.n += 16
            ld_wo = LD.n

            @on("scalar")
            def _(e, wl=ld_wo):
                e.wait_ge(LD.h, wl)
                e.copy(woutc[:], wof[:]).then_inc(AA.h, 1)
            AA.n += 1
            aa_woc = AA.n

            for grp in range(4):
                bank = grp % 2

                @on("tensor")
                def _(e, grp=grp, bank=bank, wa=aa_woc, bg=bank_guard[bank]):
                    pe_wait_bank(e, bg)
                    e.wait_ge(AA.h, wa)
                    for j in range(4):
                        idx = grp * 4 + j
                        hot_l, ht = divmod(idx, 8)
                        ii = e.transpose(PSB[bank][:, j * 128:(j + 1) * 128],
                                         woutc[:, hot_l * 1024 + ht * 128:
                                               hot_l * 1024 + (ht + 1) * 128],
                                         ident_sb[:])
                        if j == 3:
                            ii.then_inc(TT.h, 1)
                TT.n += 1
                tt_wt = TT.n

                @on("scalar")
                def _(e, chunk=chunk, grp=grp, bank=bank, wt=tt_wt):
                    e.wait_ge(TT.h, wt)
                    for j in range(4):
                        idx = grp * 4 + j
                        hot_l, ht = divmod(idx, 8)
                        hot = chunk * 2 + hot_l
                        ii = e.copy(woutT[:, ht * 1024 + hot * 128:
                                          ht * 1024 + (hot + 1) * 128],
                                    PSB[bank][:, j * 128:(j + 1) * 128])
                        if j == 3:
                            ii.then_inc(AA.h, 1)
                AA.n += 1
                guard_bank(bank, AA)
        aa_woutT = AA.n

        @on("sync")
        def _(e, wc=CY.n, wt=TT.n):
            e.wait_ge(CY.h, wc)
            e.wait_ge(TT.h, wt)
            _, pidTOK = get_pid(e)
            src = bass.AP(tensor=agy_dst.tensor, offset=pidTOK,
                          ap=[[NC * TOK, 128], [128 * NC * TOK, 8], [1, TOK]])
            e.dma_start(out=gyret[:, 0:8 * TOK].rearrange(
                "p (j c) -> p j c", c=TOK), in_=src).then_inc(LD.h, 16)
        LD.n += 16
        ld_gyret = LD.n

        for t in range(NT):
            xb = t % 2
            ob = t % 2

            @on("sync")
            def _(e, t=t, xb=xb, wv=VV.n):
                if t >= 2:
                    e.wait_ge(VV.h, wv)
                e.dma_start(out=xr_sb[xb][:],
                            in_=x_in[t * 128:(t + 1) * 128, :]).then_inc(LD.h, 16)
            LD.n += 16
            ld_xr = LD.n

            for ch in range(2):
                bank = 4 + ch

                @on("tensor")
                def _(e, t=t, ch=ch, bank=bank, wl=ld_gyret, wa=aa_woutT,
                      bg=bank_guard[bank]):
                    pe_wait_bank(e, bg)
                    if t == 0 and ch == 0:
                        e.wait_ge(LD.h, wl)
                        e.wait_ge(AA.h, wa)
                    e.matmul(PS[bank][:, 0:512], ones1_sb[:],
                             bout_sb[:, ch * 512:(ch + 1) * 512],
                             start=True, stop=False)
                    for ht in range(8):
                        ii = e.matmul(PS[bank][:, 0:512],
                                      gyret[:, ht * TOK + t * 128:
                                            ht * TOK + (t + 1) * 128],
                                      woutT[:, ht * 1024 + ch * 512:
                                            ht * 1024 + (ch + 1) * 512],
                                      start=False, stop=(ht == 7))
                        if ht == 7:
                            ii.then_inc(TT.h, 1)
                TT.n += 1
            tt_s = TT.n

            @on("vector")
            def _(e, t=t, xb=xb, ob=ob, wt=tt_s, wl=ld_xr, wa=aa_ffn,
                  ws=ST.n):
                e.wait_ge(TT.h, wt)
                e.wait_ge(LD.h, wl)
                if t == 0:
                    e.wait_ge(AA.h, wa)
                if t >= 2:
                    e.wait_ge(ST.h, ws)
                e.tensor_tensor(sg_f[:, 0:512], PS[4][:, 0:512],
                                g_sb[:, t * 1024:t * 1024 + 512], ALU.mult)
                e.tensor_tensor(sg_f[:, 512:1024], PS[5][:, 0:512],
                                g_sb[:, t * 1024 + 512:(t + 1) * 1024],
                                ALU.mult)
                e.tensor_tensor(out_f[ob][:], xr_sb[xb][:], sg_f[:], ALU.add
                                ).then_inc(VV.h, 1)
            VV.n += 1
            guard_bank(4, VV)
            guard_bank(5, VV)
            vv_out = VV.n

            @on("sync")
            def _(e, t=t, ob=ob, wv=vv_out):
                e.wait_ge(VV.h, wv)
                e.dma_start(out=out_d[t * 128:(t + 1) * 128, :],
                            in_=out_f[ob][:]).then_inc(ST.h, 16)
            ST.n += 16

        @on("sync")
        def _(e, ws=ST.n):
            e.wait_ge(ST.h, ws)

        p4.close()

        FLV = root.enter_context(nc.semaphore("FLV"))
        FLA = root.enter_context(nc.semaphore("FLA"))
        guard = (VV.h, AA.h)
        with nc.Block() as block:
            for eng_name, meth in (("sync", block.sync),
                                   ("vector", block.vector),
                                   ("scalar", block.scalar),
                                   ("tensor", block.tensor),
                                   ("gpsimd", block.gpsimd)):
                def runner(eng, eng_name=eng_name):
                    if eng_name == "vector":
                        eng = EngProxy(eng, FLV, [0], guard)
                    elif eng_name == "scalar":
                        eng = EngProxy(eng, FLA, [0], guard)
                    for en, fn in SCHED:
                        if en == eng_name:
                            fn(eng)
                meth(runner)

    import bass_rust as _bass_rust
    _bass_rust.generate_event_semaphores(nc)
    mybir.codegen_inst_isa_subclasses(nc)
    return nc, dbg_d


_NC_CACHE = {}


def _get_nc(dbg=()):
    key = tuple(sorted(dbg))
    if key not in _NC_CACHE:
        _NC_CACHE[key] = build_core(dbg=key)
    return _NC_CACHE[key]


def kernel(L0_attn_u, L0_attn_log_dt, L0_attn_log_w_real, L0_attn_w_imag,
           L0_attn_C_real, L0_attn_C_imag, L0_attn_D, L0_attn_W_out,
           L0_attn_b_out, L0_ffn_w_gate, L0_ffn_w_up, L0_ffn_w_down,
           _dbg=()):
    nc, dbg_d = _get_nc(_dbg)

    xf = np.ascontiguousarray(
        np.asarray(L0_attn_u, np.float32).reshape(16384, H))
    ident = np.eye(128, dtype=ml_dtypes.bfloat16)
    jrev = np.eye(128, dtype=ml_dtypes.bfloat16)[::-1].copy()
    ones1 = np.ones((1, 128), np.float32)
    dramp = np.arange(129, dtype=np.float32).reshape(1, 129)
    logdt = np.asarray(L0_attn_log_dt, np.float32).reshape(1024, 1)
    dcol = np.asarray(L0_attn_D, np.float32).reshape(1024, 1)
    cre = np.ascontiguousarray(
        np.asarray(L0_attn_C_real, np.float32).reshape(1024, 32))
    cim = np.ascontiguousarray(
        np.asarray(L0_attn_C_imag, np.float32).reshape(1024, 32))
    wlr = np.asarray(L0_attn_log_w_real, np.float32).reshape(1, 32)
    wim = np.asarray(L0_attn_w_imag, np.float32).reshape(1, 32)
    wout = np.ascontiguousarray(np.asarray(L0_attn_W_out, np.float32))
    bout = np.asarray(L0_attn_b_out, np.float32).reshape(1, 1024)
    wg = np.asarray(L0_ffn_w_gate, np.float32)
    wu = np.asarray(L0_ffn_w_up, np.float32)
    wd = np.asarray(L0_ffn_w_down, np.float32)

    in_maps = []
    for c in range(NC):
        if c < 6:
            lo, hi = 3 * c * 128, (3 * c + 3) * 128
        else:
            lo = (18 + (c - 6) * 2) * 128
            hi = lo + 256
        wgs = np.zeros((WSH, 1024), np.float32)
        wus = np.zeros((WSH, 1024), np.float32)
        wds = np.zeros((1024, WSH), np.float32)
        wgs[0:hi - lo] = wg[lo:hi]
        wus[0:hi - lo] = wu[lo:hi]
        wds[:, 0:hi - lo] = wd[:, lo:hi]
        in_maps.append(dict(
            x_in=np.ascontiguousarray(xf[c * TOK:(c + 1) * TOK]),
            ident_in=ident, jrev_in=jrev, ones1_in=ones1, dramp_in=dramp,
            logdt_in=np.ascontiguousarray(logdt[c * 128:(c + 1) * 128]),
            dcol_in=np.ascontiguousarray(dcol[c * 128:(c + 1) * 128]),
            cre_in=np.ascontiguousarray(cre[c * 128:(c + 1) * 128]),
            cim_in=np.ascontiguousarray(cim[c * 128:(c + 1) * 128]),
            wlr_in=wlr, wim_in=wim, wout_in=wout, bout_in=bout,
            wg_in=wgs, wu_in=wus, wd_in=wds,
        ))

    res = run_bass_kernel_spmd(nc, in_maps, core_ids=list(range(NC)))
    out = np.concatenate([res.results[c]["out"] for c in range(NC)], axis=0)
    if _dbg:
        return out.reshape(4, 4096, H), res
    return out.reshape(4, 4096, H)
